# version 01 (final)
# baseline (speedup 1.0000x reference)
import contextlib
import math

import numpy as np

import concourse.bass as bass
import concourse.mybir as mybir
from concourse.bass_utils import run_bass_kernel_spmd

F32 = mybir.dt.float32
AF = mybir.ActivationFunctionType
AL = mybir.AluOpType
AX = mybir.AxisListType

NCORE = 8
TC = 2048
D = 1024
EPS = 1e-5


class Prog:

    def __init__(self):
        self.nc = bass.Bass("TRN2", target_bir_lowering=False)
        self.st = contextlib.ExitStack()
        self.sem = self.st.enter_context(self.nc.semaphore("ord"))
        self.cnt = 0
        self.barv = 0
        self.last_wait = {}
        self.block = None

    def dram(self, name, shape, kind):
        return self.nc.dram_tensor(name, list(shape), F32, kind=kind).ap()

    def sb(self, name, p, f, dt=F32):
        return self.st.enter_context(self.nc.sbuf_tensor(name, [p, f], dt))

    def ps(self, name, p, f):
        return self.st.enter_context(self.nc.psum_tensor(name, [p, f], F32))

    def bar(self):
        self.barv = self.cnt

    def _pre(self, key, eng):
        if self.last_wait.get(key, -1) < self.barv:
            eng.wait_ge(self.sem, self.barv)
            self.last_wait[key] = self.barv

    def dma(self, dst, src, slow=False):
        e = self.nc.sync
        self._pre("sync", e)
        e.dma_start(out=dst, in_=src,
                    allow_slow_non_contiguous=slow).then_inc(self.sem, 16)
        self.cnt += 16

    def mm(self, out, lhsT, rhs, start=True, stop=True):
        e = self.nc.tensor
        self._pre("tensor", e)
        e.matmul(out, lhsT, rhs, start=start, stop=stop).then_inc(self.sem, 1)
        self.cnt += 1

    def act(self, out, in_, func, scale=None, bias=None):
        e = self.nc.scalar
        self._pre("scalar", e)
        kw = {}
        if scale is not None:
            kw["scale"] = scale
        if bias is not None:
            kw["bias"] = bias
        e.activation(out, in_, func, **kw).then_inc(self.sem, 1)
        self.cnt += 1

    def smul(self, out, in_, c):
        e = self.nc.scalar
        self._pre("scalar", e)
        e.mul(out, in_, c).then_inc(self.sem, 1)
        self.cnt += 1

    def scopy(self, out, in_):
        e = self.nc.scalar
        self._pre("scalar", e)
        e.copy(out, in_).then_inc(self.sem, 1)
        self.cnt += 1

    def tt(self, out, a, b, op):
        e = self.nc.vector
        self._pre("vector", e)
        e.tensor_tensor(out, a, b, op).then_inc(self.sem, 1)
        self.cnt += 1

    def tsc(self, out, in0, s1, op0, s2=None, op1=None):
        e = self.nc.vector
        self._pre("vector", e)
        if op1 is None:
            e.tensor_scalar(out, in0, s1, None, op0).then_inc(self.sem, 1)
        else:
            e.tensor_scalar(out, in0, s1, s2, op0, op1).then_inc(self.sem, 1)
        self.cnt += 1

    def rmax(self, out, in_):
        e = self.nc.vector
        self._pre("vector", e)
        e.reduce_max(out, in_, AX.X).then_inc(self.sem, 1)
        self.cnt += 1

    def recip(self, out, in_):
        e = self.nc.vector
        self._pre("vector", e)
        e.reciprocal(out, in_).then_inc(self.sem, 1)
        self.cnt += 1

    def memset(self, ap, c):
        e = self.nc.vector
        self._pre("vector", e)
        e.memset(ap, c).then_inc(self.sem, 1)
        self.cnt += 1

    def finish(self):
        self.bar()
        self.nc.sync.wait_ge(self.sem, self.cnt)
        self.st.close()
        return self.nc


def _linear_fm(p, out_dram, wsb_chunks, x_chunks, n_out, tb0, tbw, psum,
               post=None):
    nk = len(x_chunks)
    for nt in range(n_out // 128):
        for k in range(nk):
            p.mm(psum[:128, :tbw], wsb_chunks[k][:, nt * 128:(nt + 1) * 128],
                 x_chunks[k][:, :tbw], start=(k == 0), stop=(k == nk - 1))
        p.bar()
        if post is None:
            p.scopy(psum[:128, :tbw], psum[:128, :tbw])
        post(nt, psum)
        p.bar()


def build_p1():
    p = Prog()
    xT = p.dram("xT", (D, TC), "ExternalInput")
    wqkv = p.dram("wqkv", (D, 3072), "ExternalInput")
    wgT = p.dram("wgT", (D, 8), "ExternalInput")
    xnT = p.dram("xnT", (D, TC), "ExternalOutput")
    qkvT = p.dram("qkvT", (3072, TC), "ExternalOutput")
    lgT = p.dram("lgT", (8, TC), "ExternalOutput")

    w_sb = [p.sb(f"w{k}", 128, 3072) for k in range(8)]
    wg_sb = [p.sb(f"wg{k}", 128, 8) for k in range(8)]
    ones128 = p.sb("o128", 128, 1)
    ones1 = p.sb("o1", 1, 128)
    x_sb = [p.sb(f"x{k}", 128, 512) for k in range(8)]
    xs_sb = p.sb("xs", 128, 512)
    xn_sb = [p.sb(f"xn{k}", 128, 512) for k in range(8)]
    row = p.sb("row", 1, 512)
    row2 = p.sb("row2", 1, 512)
    row3 = p.sb("row3", 1, 512)
    ev = p.sb("ev", 128, 512)
    ps_sum = p.ps("ps_sum", 128, 512)
    ps_sq = p.ps("ps_sq", 128, 512)
    ps_bm = p.ps("ps_bm", 128, 512)
    ps_br = p.ps("ps_br", 128, 512)
    ps_mm = p.ps("ps_mm", 128, 512)

    p.memset(ones128[:, :], 1.0)
    p.memset(ones1[:, :], 1.0)
    for k in range(8):
        p.dma(w_sb[k][:, :], wqkv[k * 128:(k + 1) * 128, :])
        p.dma(wg_sb[k][:, :], wgT[k * 128:(k + 1) * 128, :])
    p.bar()
    for tb in range(TC // 512):
        c0 = tb * 512
        for k in range(8):
            p.dma(x_sb[k][:, :], xT[k * 128:(k + 1) * 128, c0:c0 + 512])
        p.bar()
        for k in range(8):
            p.mm(ps_sum[0:1, :], ones128[:, :], x_sb[k][:, :],
                 start=(k == 0), stop=(k == 7))
        for k in range(8):
            p.tt(xs_sb[:, :], x_sb[k][:, :], x_sb[k][:, :], AL.mult)
            p.bar()
            p.mm(ps_sq[0:1, :], ones128[:, :], xs_sb[:, :],
                 start=(k == 0), stop=(k == 7))
            p.bar()
        p.smul(row[:, :], ps_sum[0:1, :], 1.0 / D)
        p.smul(row2[:, :], ps_sq[0:1, :], 1.0 / D)
        p.bar()
        p.tt(row3[:, :], row[:, :], row[:, :], AL.mult)
        p.bar()
        p.tt(row2[:, :], row2[:, :], row3[:, :], AL.subtract)
        p.bar()
        p.tsc(row2[:, :], row2[:, :], EPS, AL.add)
        p.bar()
        p.act(row2[:, :], row2[:, :], AF.Sqrt)
        p.bar()
        p.recip(row2[:, :], row2[:, :])
        p.bar()
        p.mm(ps_bm[:, :], ones1[:, :], row[:, :])
        p.mm(ps_br[:, :], ones1[:, :], row2[:, :])
        p.bar()
        for k in range(8):
            p.tt(xn_sb[k][:, :], x_sb[k][:, :], ps_bm[:, :], AL.subtract)
            p.bar()
            p.tt(xn_sb[k][:, :], xn_sb[k][:, :], ps_br[:, :], AL.mult)
            p.bar()
            p.dma(xnT[k * 128:(k + 1) * 128, c0:c0 + 512], xn_sb[k][:, :])
        p.bar()
        for nt in range(24):
            for k in range(8):
                p.mm(ps_mm[:, :], w_sb[k][:, nt * 128:(nt + 1) * 128],
                     xn_sb[k][:, :], start=(k == 0), stop=(k == 7))
            p.bar()
            p.scopy(ev[:, :], ps_mm[:, :])
            p.bar()
            p.dma(qkvT[nt * 128:(nt + 1) * 128, c0:c0 + 512], ev[:, :])
            p.bar()
        for k in range(8):
            p.mm(ps_mm[0:8, :], wg_sb[k][:, :], xn_sb[k][:, :],
                 start=(k == 0), stop=(k == 7))
        p.bar()
        p.scopy(ev[0:8, :], ps_mm[0:8, :])
        p.bar()
        p.dma(lgT[:, c0:c0 + 512], ev[0:8, :])
        p.bar()
    return p


def build_p2():
    p = Prog()
    qT = p.dram("qT", (D, TC), "ExternalInput")
    KT = p.dram("KT", (D, 8192), "ExternalInput")
    V = p.dram("V", (8192, D), "ExternalInput")
    oT = p.dram("oT", (D, TC), "ExternalOutput")

    kt = p.sb("kt", 64, 8192)
    qt = p.sb("qt", 64, TC)
    vh = p.sb("vh", 128, 65 * 64)
    pt = p.sb("pt", 128, 2048)
    linv = p.sb("linv", 1, 512)
    bsb = p.sb("bsb", 64, 512)
    osb = p.sb("osb", 64, 512)
    ones1 = p.sb("o1", 1, 64)
    ps_s = p.ps("ps_s", 128, 2048)
    ps_o = p.ps("ps_o", 128, 512)
    ps_b = p.ps("ps_b", 128, 512)

    p.memset(ones1[:, :], 1.0)
    for h in range(16):
        p.bar()
        p.dma(kt[:, :], KT[h * 64:(h + 1) * 64, :])
        p.dma(qt[:, :], qT[h * 64:(h + 1) * 64, :])
        p.memset(vh[:, :], 1.0)
        p.bar()
        for kb in range(64):
            p.dma(vh[:, kb * 65:kb * 65 + 64],
                  V[kb * 128:(kb + 1) * 128, h * 64:(h + 1) * 64])
        p.bar()
        for qb in range(TC // 512):
            q0 = qb * 512
            for g in range(16):
                p.bar()
                for j in range(4):
                    kb = g * 4 + j
                    p.mm(ps_s[:, j * 512:(j + 1) * 512],
                         kt[:, kb * 128:(kb + 1) * 128],
                         qt[:, q0:q0 + 512])
                p.bar()
                p.act(pt[:, :], ps_s[:, :], AF.Exp, scale=1.0 / 32.0)
                p.bar()
                for j in range(4):
                    kb = g * 4 + j
                    p.mm(ps_o[0:65, :], vh[:, kb * 65:kb * 65 + 65],
                         pt[:, j * 512:(j + 1) * 512],
                         start=(kb == 0), stop=(kb == 63))
            p.bar()
            p.recip(linv[:, :], ps_o[64:65, :])
            p.bar()
            p.mm(ps_b[0:64, :], ones1[:, :], linv[:, :])
            p.bar()
            p.scopy(bsb[:, :], ps_b[0:64, :])
            p.bar()
            p.tt(osb[:, :], ps_o[0:64, :], bsb[:, :], AL.mult)
            p.bar()
            p.dma(oT[h * 64:(h + 1) * 64, q0:q0 + 512], osb[:, :])
    return p


def build_p3():
    p = Prog()
    oT = p.dram("oT", (D, TC), "ExternalInput")
    wout = p.dram("wout", (D, D), "ExternalInput")
    lg = p.dram("lg", (TC, 8), "ExternalInput")
    sT = p.dram("sT", (D, TC), "ExternalOutput")
    cmb = p.dram("cmb", (TC, 8), "ExternalOutput")

    w_sb = [p.sb(f"w{k}", 128, D) for k in range(8)]
    x_sb = [p.sb(f"x{k}", 128, 512) for k in range(8)]
    ev = p.sb("ev", 128, 512)
    ps_mm = p.ps("ps_mm", 128, 512)
    lg_sb = p.sb("lgs", 128, 8)
    m1 = p.sb("m1", 128, 1)
    m2 = p.sb("m2", 128, 1)
    dd = p.sb("dd", 128, 1)
    p1s = p.sb("p1s", 128, 1)
    p2s = p.sb("p2s", 128, 1)
    eq1 = p.sb("eq1", 128, 8)
    eq2 = p.sb("eq2", 128, 8)
    msk = p.sb("msk", 128, 8)
    csb = p.sb("csb", 128, 8)

    for k in range(8):
        p.dma(w_sb[k][:, :], wout[k * 128:(k + 1) * 128, :])
    p.bar()
    for tb in range(TC // 512):
        c0 = tb * 512
        for k in range(8):
            p.dma(x_sb[k][:, :], oT[k * 128:(k + 1) * 128, c0:c0 + 512])
        p.bar()
        for nt in range(8):
            for k in range(8):
                p.mm(ps_mm[:, :], w_sb[k][:, nt * 128:(nt + 1) * 128],
                     x_sb[k][:, :], start=(k == 0), stop=(k == 7))
            p.bar()
            p.scopy(ev[:, :], ps_mm[:, :])
            p.bar()
            p.dma(sT[nt * 128:(nt + 1) * 128, c0:c0 + 512], ev[:, :])
            p.bar()
    for t in range(TC // 128):
        r0 = t * 128
        p.bar()
        p.dma(lg_sb[:, :], lg[r0:r0 + 128, :])
        p.bar()
        p.rmax(m1[:, :], lg_sb[:, :])
        p.bar()
        p.tsc(eq1[:, :], lg_sb[:, :], m1[:, :], AL.is_equal)
        p.bar()
        p.tsc(msk[:, :], eq1[:, :], -1e30, AL.mult)
        p.bar()
        p.tt(msk[:, :], msk[:, :], lg_sb[:, :], AL.add)
        p.bar()
        p.rmax(m2[:, :], msk[:, :])
        p.bar()
        p.tsc(eq2[:, :], lg_sb[:, :], m2[:, :], AL.is_equal)
        p.tt(dd[:, :], m2[:, :], m1[:, :], AL.subtract)
        p.bar()
        p.act(p2s[:, :], dd[:, :], AF.Sigmoid)
        p.act(p1s[:, :], dd[:, :], AF.Sigmoid, scale=-1.0)
        p.bar()
        p.tsc(eq1[:, :], eq1[:, :], p1s[:, :], AL.mult)
        p.bar()
        p.tsc(eq2[:, :], eq2[:, :], p2s[:, :], AL.mult)
        p.bar()
        p.tt(csb[:, :], eq1[:, :], eq2[:, :], AL.add)
        p.bar()
        p.dma(cmb[r0:r0 + 128, :], csb[:, :])
    return p


def build_p4():
    p = Prog()
    xnT = p.dram("xnT", (D, TC), "ExternalInput")
    cT = p.dram("cT", (8, TC), "ExternalInput")
    cfc = p.dram("cfc", (8, D, 4096), "ExternalInput")
    cpr = p.dram("cpr", (8, 4096, D), "ExternalInput")
    gT = p.dram("gT", (D, TC), "ExternalOutput")

    x_sb = [p.sb(f"x{k}", 128, 512) for k in range(8)]
    xw_sb = [p.sb(f"xw{k}", 128, 512) for k in range(8)]
    acc = [p.sb(f"ac{k}", 128, 512) for k in range(8)]
    wfc = [p.sb(f"wf{k}", 128, 1024) for k in range(8)]
    wpr = [p.sb(f"wp{k}", 128, 1024) for k in range(8)]
    hbuf = [p.sb(f"hb{k}", 128, 512) for k in range(8)]
    ce = p.sb("ces0", 1, TC)
    ces = p.sb("ces1", 1, TC)
    ones1 = p.sb("o1", 1, 128)
    cbs = p.sb("cbs", 128, 512)
    ev = p.sb("ev", 128, 512)
    ps_h = p.ps("ps_h", 128, 512)
    ps_c = p.ps("ps_c", 128, 512)
    ps_p = p.ps("ps_p", 128, 512)

    p.memset(ones1[:, :], 1.0)
    p.bar()
    for tb in range(TC // 512):
        c0 = tb * 512
        p.bar()
        for k in range(8):
            p.dma(x_sb[k][:, :], xnT[k * 128:(k + 1) * 128, c0:c0 + 512])
            p.memset(acc[k][:, :], 0.0)
        p.bar()
        for e in range(8):
            p.dma(ce[:, 0:512], cT[e:e + 1, c0:c0 + 512])
            p.bar()
            p.act(ces[:, 0:512], ce[:, 0:512], AF.Sqrt)
            p.bar()
            p.mm(ps_c[:, :], ones1[:, :], ces[:, 0:512])
            p.bar()
            p.scopy(cbs[:, :], ps_c[:, :])
            p.bar()
            for k in range(8):
                p.tt(xw_sb[k][:, :], x_sb[k][:, :], cbs[:, :], AL.mult)
            p.bar()
            for fg in range(4):
                f0 = fg * 1024
                p.bar()
                for k in range(8):
                    p.dma(wfc[k][:, :], cfc[e, k * 128:(k + 1) * 128,
                                            f0:f0 + 1024])
                for k in range(8):
                    p.dma(wpr[k][:, :], cpr[e, f0 + k * 128:f0 + (k + 1) * 128,
                                            :])
                p.bar()
                for ft in range(8):
                    for k in range(8):
                        p.mm(ps_h[:, :], wfc[k][:, ft * 128:(ft + 1) * 128],
                             xw_sb[k][:, :], start=(k == 0), stop=(k == 7))
                    p.bar()
                    p.act(hbuf[ft][:, :], ps_h[:, :], AF.Relu)
                    p.bar()
                    p.tt(hbuf[ft][:, :], hbuf[ft][:, :], hbuf[ft][:, :],
                         AL.mult)
                    p.bar()
                for nt in range(8):
                    for k in range(8):
                        p.mm(ps_p[:, :], wpr[k][:, nt * 128:(nt + 1) * 128],
                             hbuf[k][:, :], start=(k == 0), stop=(k == 7))
                    p.bar()
                    p.tt(acc[nt][:, :], acc[nt][:, :], ps_p[:, :], AL.add)
                    p.bar()
        for k in range(8):
            p.act(ev[:, :], acc[k][:, :], AF.Silu)
            p.bar()
            p.dma(gT[k * 128:(k + 1) * 128, c0:c0 + 512], ev[:, :])
        p.bar()
    return p


def build_p5():
    p = Prog()
    xT = p.dram("xT", (D, TC), "ExternalInput")
    sT = p.dram("sT", (D, TC), "ExternalInput")
    gT = p.dram("gT", (D, TC), "ExternalInput")
    wqkvT = p.dram("wqkvT", (D, 3072), "ExternalInput")
    nsc = p.dram("nsc", (D,), "ExternalInput")
    gwT = p.dram("gwT", (D, 32), "ExternalInput")
    gb = p.dram("gb", (32,), "ExternalInput")
    x1T = p.dram("x1T", (D, TC), "ExternalOutput")
    xn1T = p.dram("xn1T", (D, TC), "ExternalOutput")
    tT = p.dram("tT", (D, TC), "ExternalOutput")
    glT = p.dram("glT", (32, TC), "ExternalOutput")
    qkv1T = p.dram("qkv1T", (3072, TC), "ExternalOutput")

    w_sb = [p.sb(f"w{k}", 128, 3072) for k in range(8)]
    gw_sb = [p.sb(f"gw{k}", 128, 32) for k in range(8)]
    nsc_sb = p.sb("nscs", 128, 8)
    gb_sb = p.sb("gbs", 32, 1)
    ones128 = p.sb("o128", 128, 1)
    ones1 = p.sb("o1", 1, 128)
    x_sb = [p.sb(f"x{k}", 128, 512) for k in range(8)]
    s_sb = p.sb("s", 128, 512)
    g_sb = p.sb("g", 128, 512)
    xn_sb = [p.sb(f"xn{k}", 128, 512) for k in range(8)]
    t_sb = [p.sb(f"t{k}", 128, 512) for k in range(8)]
    xs_sb = p.sb("xs", 128, 512)
    row = p.sb("row", 1, 512)
    row2 = p.sb("row2", 1, 512)
    row3 = p.sb("row3", 1, 512)
    ev = p.sb("ev", 128, 512)
    ps_sum = p.ps("ps_sum", 128, 512)
    ps_sq = p.ps("ps_sq", 128, 512)
    ps_bm = p.ps("ps_bm", 128, 512)
    ps_br = p.ps("ps_br", 128, 512)
    ps_mm = p.ps("ps_mm", 128, 512)

    p.memset(ones128[:, :], 1.0)
    p.memset(ones1[:, :], 1.0)
    p.dma(nsc_sb[:, :], nsc.rearrange("(f p) -> p f", p=128), slow=True)
    p.dma(gb_sb[:, :], gb.rearrange("(e o) -> e o", o=1), slow=True)
    for k in range(8):
        p.dma(w_sb[k][:, :], wqkvT[k * 128:(k + 1) * 128, :])
        p.dma(gw_sb[k][:, :], gwT[k * 128:(k + 1) * 128, :])
    p.bar()
    for tb in range(TC // 512):
        c0 = tb * 512
        for k in range(8):
            p.dma(x_sb[k][:, :], xT[k * 128:(k + 1) * 128, c0:c0 + 512])
        p.bar()
        for k in range(8):
            p.dma(s_sb[:, :], sT[k * 128:(k + 1) * 128, c0:c0 + 512])
            p.dma(g_sb[:, :], gT[k * 128:(k + 1) * 128, c0:c0 + 512])
            p.bar()
            p.tt(s_sb[:, :], s_sb[:, :], g_sb[:, :], AL.mult)
            p.bar()
            p.tt(x_sb[k][:, :], x_sb[k][:, :], s_sb[:, :], AL.add)
            p.bar()
            p.dma(x1T[k * 128:(k + 1) * 128, c0:c0 + 512], x_sb[k][:, :])
        p.bar()
        for k in range(8):
            p.mm(ps_sum[0:1, :], ones128[:, :], x_sb[k][:, :],
                 start=(k == 0), stop=(k == 7))
        for k in range(8):
            p.tt(xs_sb[:, :], x_sb[k][:, :], x_sb[k][:, :], AL.mult)
            p.bar()
            p.mm(ps_sq[0:1, :], ones128[:, :], xs_sb[:, :],
                 start=(k == 0), stop=(k == 7))
            p.bar()
        p.smul(row[:, :], ps_sum[0:1, :], 1.0 / D)
        p.smul(row2[:, :], ps_sq[0:1, :], 1.0 / D)
        p.bar()
        p.tt(row3[:, :], row[:, :], row[:, :], AL.mult)
        p.bar()
        p.tt(row2[:, :], row2[:, :], row3[:, :], AL.subtract)
        p.bar()
        p.tsc(row2[:, :], row2[:, :], EPS, AL.add)
        p.bar()
        p.act(row2[:, :], row2[:, :], AF.Sqrt)
        p.bar()
        p.recip(row2[:, :], row2[:, :])
        p.bar()
        p.mm(ps_bm[:, :], ones1[:, :], row[:, :])
        p.mm(ps_br[:, :], ones1[:, :], row2[:, :])
        p.bar()
        for k in range(8):
            p.tt(xn_sb[k][:, :], x_sb[k][:, :], ps_bm[:, :], AL.subtract)
            p.bar()
            p.tt(xn_sb[k][:, :], xn_sb[k][:, :], ps_br[:, :], AL.mult)
            p.bar()
            p.dma(xn1T[k * 128:(k + 1) * 128, c0:c0 + 512], xn_sb[k][:, :])
        p.bar()
        for k in range(8):
            p.tt(xs_sb[:, :], xn_sb[k][:, :], xn_sb[k][:, :], AL.mult)
            p.bar()
            p.mm(ps_sq[0:1, :], ones128[:, :], xs_sb[:, :],
                 start=(k == 0), stop=(k == 7))
            p.bar()
        p.smul(row2[:, :], ps_sq[0:1, :], 1.0 / D)
        p.bar()
        p.tsc(row2[:, :], row2[:, :], EPS, AL.add)
        p.bar()
        p.act(row2[:, :], row2[:, :], AF.Sqrt)
        p.bar()
        p.recip(row2[:, :], row2[:, :])
        p.bar()
        p.mm(ps_br[:, :], ones1[:, :], row2[:, :])
        p.bar()
        for k in range(8):
            p.tt(t_sb[k][:, :], xn_sb[k][:, :], ps_br[:, :], AL.mult)
            p.bar()
            p.tsc(t_sb[k][:, :], t_sb[k][:, :], nsc_sb[:, k:k + 1], AL.mult)
            p.bar()
            p.dma(tT[k * 128:(k + 1) * 128, c0:c0 + 512], t_sb[k][:, :])
        p.bar()
        for nt in range(24):
            for k in range(8):
                p.mm(ps_mm[:, :], w_sb[k][:, nt * 128:(nt + 1) * 128],
                     xn_sb[k][:, :], start=(k == 0), stop=(k == 7))
            p.bar()
            p.scopy(ev[:, :], ps_mm[:, :])
            p.bar()
            p.dma(qkv1T[nt * 128:(nt + 1) * 128, c0:c0 + 512], ev[:, :])
            p.bar()
        for k in range(8):
            p.mm(ps_mm[0:32, :], gw_sb[k][:, :], t_sb[k][:, :],
                 start=(k == 0), stop=(k == 7))
        p.bar()
        p.tsc(ev[0:32, :], ps_mm[0:32, :], gb_sb[:, :], AL.add)
        p.bar()
        p.dma(glT[:, c0:c0 + 512], ev[0:32, :])
        p.bar()
    return p


def build_p6():
    p = Prog()
    qT = p.dram("qT", (D, TC), "ExternalInput")
    KW = p.dram("KW", (4, D, 1280), "ExternalInput")
    VW = p.dram("VW", (4, 1280, D), "ExternalInput")
    MK = p.dram("MK", (4, 10, 128, 512), "ExternalInput")
    oT = p.dram("oT", (D, TC), "ExternalOutput")

    kt = p.sb("kt", 64, 1280)
    qt = p.sb("qt", 64, TC)
    vh = p.sb("vh", 128, 65 * 10)
    msk = [p.sb(f"mk{i}", 128, 512) for i in range(10)]
    ssb = p.sb("ssb", 128, 512)
    pt = p.sb("pt", 128, 512)
    linv = p.sb("linv", 1, 512)
    bsb = p.sb("bsb", 64, 512)
    osb = p.sb("osb", 64, 512)
    ones1 = p.sb("o1", 1, 64)
    ps_s = p.ps("ps_s", 128, 512)
    ps_o = p.ps("ps_o", 128, 512)
    ps_b = p.ps("ps_b", 128, 512)

    p.memset(ones1[:, :], 1.0)
    for qb in range(4):
        p.bar()
        for i in range(10):
            p.dma(msk[i][:, :], MK[qb, i, :, :])
        p.bar()
        for h in range(16):
            p.bar()
            p.dma(kt[:, :], KW[qb, h * 64:(h + 1) * 64, :])
            p.dma(qt[:, 0:512], qT[h * 64:(h + 1) * 64, qb * 512:(qb + 1) * 512])
            p.memset(vh[:, :], 1.0)
            p.bar()
            for kb in range(10):
                p.dma(vh[:, kb * 65:kb * 65 + 64],
                      VW[qb, kb * 128:(kb + 1) * 128, h * 64:(h + 1) * 64])
            p.bar()
            for kb in range(10):
                p.mm(ps_s[:, :], kt[:, kb * 128:(kb + 1) * 128], qt[:, 0:512])
                p.bar()
                p.tt(ssb[:, :], ps_s[:, :], msk[kb][:, :], AL.add)
                p.bar()
                p.act(pt[:, :], ssb[:, :], AF.Exp, scale=0.125)
                p.bar()
                p.mm(ps_o[0:65, :], vh[:, kb * 65:kb * 65 + 65], pt[:, :],
                     start=(kb == 0), stop=(kb == 9))
                p.bar()
            p.recip(linv[:, :], ps_o[64:65, :])
            p.bar()
            p.mm(ps_b[0:64, :], ones1[:, :], linv[:, :])
            p.bar()
            p.scopy(bsb[:, :], ps_b[0:64, :])
            p.bar()
            p.tt(osb[:, :], ps_o[0:64, :], bsb[:, :], AL.mult)
            p.bar()
            p.dma(oT[h * 64:(h + 1) * 64, qb * 512:(qb + 1) * 512], osb[:, :])
    return p


def build_p7():
    p = Prog()
    oT = p.dram("oT", (D, TC), "ExternalInput")
    wout = p.dram("wout", (D, D), "ExternalInput")
    gl = p.dram("gl", (TC, 32), "ExternalInput")
    sT = p.dram("sT", (D, TC), "ExternalOutput")
    cmb = p.dram("cmb", (TC, 32), "ExternalOutput")

    w_sb = [p.sb(f"w{k}", 128, D) for k in range(8)]
    x_sb = [p.sb(f"x{k}", 128, 512) for k in range(8)]
    ev = p.sb("ev", 128, 512)
    ps_mm = p.ps("ps_mm", 128, 512)
    lg_sb = p.sb("lgs", 128, 32)
    wrk = p.sb("wrk", 128, 32)
    eqs = [p.sb(f"eq{i}", 128, 32) for i in range(4)]
    ms = [p.sb(f"m{i}", 128, 1) for i in range(4)]
    es = [p.sb(f"e{i}", 128, 1) for i in range(4)]
    ssum = p.sb("ssum", 128, 1)
    sinv = p.sb("sinv", 128, 1)
    csb = p.sb("csb", 128, 32)

    for k in range(8):
        p.dma(w_sb[k][:, :], wout[k * 128:(k + 1) * 128, :])
    p.bar()
    for tb in range(TC // 512):
        c0 = tb * 512
        for k in range(8):
            p.dma(x_sb[k][:, :], oT[k * 128:(k + 1) * 128, c0:c0 + 512])
        p.bar()
        for nt in range(8):
            for k in range(8):
                p.mm(ps_mm[:, :], w_sb[k][:, nt * 128:(nt + 1) * 128],
                     x_sb[k][:, :], start=(k == 0), stop=(k == 7))
            p.bar()
            p.scopy(ev[:, :], ps_mm[:, :])
            p.bar()
            p.dma(sT[nt * 128:(nt + 1) * 128, c0:c0 + 512], ev[:, :])
            p.bar()
    for t in range(TC // 128):
        r0 = t * 128
        p.bar()
        p.dma(lg_sb[:, :], gl[r0:r0 + 128, :])
        p.bar()
        p.scopy(wrk[:, :], lg_sb[:, :])
        p.bar()
        for i in range(4):
            p.rmax(ms[i][:, :], wrk[:, :])
            p.bar()
            p.tsc(eqs[i][:, :], lg_sb[:, :], ms[i][:, :], AL.is_equal)
            p.bar()
            if i < 3:
                p.tsc(csb[:, :], eqs[i][:, :], -1e30, AL.mult)
                p.bar()
                p.tt(wrk[:, :], wrk[:, :], csb[:, :], AL.add)
                p.bar()
        p.memset(ssum[:, :], 0.0)
        p.bar()
        for i in range(4):
            p.tt(es[i][:, :], ms[i][:, :], ms[0][:, :], AL.subtract)
            p.bar()
            p.act(es[i][:, :], es[i][:, :], AF.Exp)
            p.bar()
            p.tt(ssum[:, :], ssum[:, :], es[i][:, :], AL.add)
            p.bar()
        p.recip(sinv[:, :], ssum[:, :])
        p.bar()
        p.memset(csb[:, :], 0.0)
        p.bar()
        for i in range(4):
            p.tsc(es[i][:, :], es[i][:, :], sinv[:, :], AL.mult)
            p.bar()
            p.tsc(eqs[i][:, :], eqs[i][:, :], es[i][:, :], AL.mult)
            p.bar()
            p.tt(csb[:, :], csb[:, :], eqs[i][:, :], AL.add)
            p.bar()
        p.dma(cmb[r0:r0 + 128, :], csb[:, :])
    return p


def build_p8():
    p = Prog()
    tT = p.dram("tT", (D, TC), "ExternalInput")
    cT = p.dram("cT", (32, TC), "ExternalInput")
    w1A = p.dram("w1A", (32, D, D), "ExternalInput")
    w1B = p.dram("w1B", (32, D, D), "ExternalInput")
    b1A = p.dram("b1A", (32, D), "ExternalInput")
    b1B = p.dram("b1B", (32, D), "ExternalInput")
    w2T = p.dram("w2T", (32, D, D), "ExternalInput")
    b2 = p.dram("b2", (32, D), "ExternalInput")
    fT = p.dram("fT", (D, TC), "ExternalOutput")

    t_sb = [p.sb(f"t{k}", 128, 512) for k in range(8)]
    acc = [p.sb(f"ac{k}", 128, 512) for k in range(8)]
    wa = [p.sb(f"wa{k}", 128, D) for k in range(8)]
    wb = [p.sb(f"wb{k}", 128, D) for k in range(8)]
    w2 = [p.sb(f"w2{k}", 128, D) for k in range(8)]
    hA = [p.sb(f"hA{k}", 128, 512) for k in range(8)]
    hB = p.sb("hB", 128, 512)
    sg = p.sb("sg", 128, 512)
    bA = p.sb("bA", 128, 8)
    bB = p.sb("bB", 128, 8)
    b2sb = p.sb("b2s", 32, D)
    ctall = p.sb("ctall", 32, TC)
    ce = p.sb("ces0", 1, TC)
    cbs = p.sb("cbs", 128, 512)
    ones1 = p.sb("o1", 1, 128)
    ps_h = p.ps("ps_h", 128, 512)
    ps_c = p.ps("ps_c", 128, 512)
    ps_p = p.ps("ps_p", 128, 512)

    p.memset(ones1[:, :], 1.0)
    p.dma(b2sb[:, :], b2[:, :])
    p.dma(ctall[:, :], cT[:, :])
    p.bar()
    for tb in range(TC // 512):
        c0 = tb * 512
        p.bar()
        for k in range(8):
            p.dma(t_sb[k][:, :], tT[k * 128:(k + 1) * 128, c0:c0 + 512])
            p.memset(acc[k][:, :], 0.0)
        p.bar()
        for e in range(32):
            p.bar()
            for k in range(8):
                p.dma(wa[k][:, :], w1A[e, k * 128:(k + 1) * 128, :])
                p.dma(wb[k][:, :], w1B[e, k * 128:(k + 1) * 128, :])
                p.dma(w2[k][:, :], w2T[e, k * 128:(k + 1) * 128, :])
            p.dma(bA[:, :], b1A[e].rearrange("(f p) -> p f", p=128), slow=True)
            p.dma(bB[:, :], b1B[e].rearrange("(f p) -> p f", p=128), slow=True)
            p.dma(ce[:, 0:512], cT[e:e + 1, c0:c0 + 512])
            p.bar()
            p.mm(ps_c[:, :], ones1[:, :], ce[:, 0:512])
            p.bar()
            p.scopy(cbs[:, :], ps_c[:, :])
            p.bar()
            for ft in range(8):
                for k in range(8):
                    p.mm(ps_h[:, :], wa[k][:, ft * 128:(ft + 1) * 128],
                         t_sb[k][:, :], start=(k == 0), stop=(k == 7))
                p.bar()
                p.tsc(hA[ft][:, :], ps_h[:, :], bA[:, ft:ft + 1], AL.add,
                      7.0, AL.min)
                p.bar()
                for k in range(8):
                    p.mm(ps_h[:, :], wb[k][:, ft * 128:(ft + 1) * 128],
                         t_sb[k][:, :], start=(k == 0), stop=(k == 7))
                p.bar()
                p.tsc(hB[:, :], ps_h[:, :], bB[:, ft:ft + 1], AL.add,
                      7.0, AL.min)
                p.bar()
                p.tsc(hB[:, :], hB[:, :], -7.0, AL.max, 1.0, AL.add)
                p.bar()
                p.act(sg[:, :], hA[ft][:, :], AF.Sigmoid, scale=1.702)
                p.bar()
                p.tt(hA[ft][:, :], hA[ft][:, :], sg[:, :], AL.mult)
                p.bar()
                p.tt(hA[ft][:, :], hA[ft][:, :], hB[:, :], AL.mult)
                p.bar()
                p.tt(hA[ft][:, :], hA[ft][:, :], cbs[:, :], AL.mult)
                p.bar()
            for nt in range(8):
                for k in range(8):
                    p.mm(ps_p[:, :], w2[k][:, nt * 128:(nt + 1) * 128],
                         hA[k][:, :], start=(k == 0), stop=(k == 7))
                p.bar()
                p.tt(acc[nt][:, :], acc[nt][:, :], ps_p[:, :], AL.add)
                p.bar()
        for nt in range(8):
            p.mm(ps_p[:, :], b2sb[:, nt * 128:(nt + 1) * 128],
                 ctall[:, c0:c0 + 512])
            p.bar()
            p.tt(acc[nt][:, :], acc[nt][:, :], ps_p[:, :], AL.add)
            p.bar()
            p.dma(fT[nt * 128:(nt + 1) * 128, c0:c0 + 512], acc[nt][:, :])
        p.bar()
    return p


def build_p9():
    p = Prog()
    x1T = p.dram("x1T", (D, TC), "ExternalInput")
    s1T = p.dram("s1T", (D, TC), "ExternalInput")
    xn1T = p.dram("xn1T", (D, TC), "ExternalInput")
    fT = p.dram("fT", (D, TC), "ExternalInput")
    x2T = p.dram("x2T", (D, TC), "ExternalOutput")

    a = p.sb("a", 128, 512)
    b = p.sb("b", 128, 512)
    c = p.sb("c", 128, 512)
    d = p.sb("d", 128, 512)

    for tb in range(TC // 512):
        c0 = tb * 512
        for k in range(8):
            p.bar()
            p.dma(a[:, :], xn1T[k * 128:(k + 1) * 128, c0:c0 + 512])
            p.dma(b[:, :], fT[k * 128:(k + 1) * 128, c0:c0 + 512])
            p.dma(c[:, :], s1T[k * 128:(k + 1) * 128, c0:c0 + 512])
            p.dma(d[:, :], x1T[k * 128:(k + 1) * 128, c0:c0 + 512])
            p.bar()
            p.tt(a[:, :], a[:, :], b[:, :], AL.add)
            p.bar()
            p.act(a[:, :], a[:, :], AF.Silu)
            p.bar()
            p.tt(a[:, :], a[:, :], c[:, :], AL.mult)
            p.bar()
            p.tt(a[:, :], a[:, :], d[:, :], AL.add)
            p.bar()
            p.dma(x2T[k * 128:(k + 1) * 128, c0:c0 + 512], a[:, :])
    return p


def _run(build, maps):
    p = build()
    nc = p.finish()
    res = run_bass_kernel_spmd(nc, maps, core_ids=list(range(NCORE)))
    return res.results


def _cols(full):
    return [np.ascontiguousarray(full[:, c * TC:(c + 1) * TC])
            for c in range(NCORE)]


def kernel(L0_attn_x, L0_attn_w_qkv, L0_attn_w_out, L0_ffn_w_g, L0_ffn_c_fc,
           L0_ffn_c_proj, L1_attn_W_qkv, L1_attn_W_out, L1_ffn_norm_scale,
           L1_ffn_gate_w, L1_ffn_gate_b, L1_ffn_w1, L1_ffn_b1, L1_ffn_w2,
           L1_ffn_b2):
    f = np.float32
    xT = np.ascontiguousarray(L0_attn_x.reshape(16384, D).T).astype(f)
    wqkv = np.ascontiguousarray(L0_attn_w_qkv, dtype=f)
    wgT = np.ascontiguousarray(np.asarray(L0_ffn_w_g, dtype=f).T)

    r1 = _run(build_p1, [{"xT": s, "wqkv": wqkv, "wgT": wgT}
                         for s in _cols(xT)])
    xnT = np.concatenate([r["xnT"] for r in r1], axis=1)
    qkvT = np.concatenate([r["qkvT"] for r in r1], axis=1)
    lgT = np.concatenate([r["lgT"] for r in r1], axis=1)

    QT, KT, VT = qkvT[0:D], qkvT[D:2 * D], qkvT[2 * D:3 * D]
    maps2 = []
    for c in range(NCORE):
        b = c // 4
        maps2.append({
            "qT": np.ascontiguousarray(QT[:, c * TC:(c + 1) * TC]),
            "KT": np.ascontiguousarray(KT[:, b * 8192:(b + 1) * 8192]),
            "V": np.ascontiguousarray(VT[:, b * 8192:(b + 1) * 8192].T),
        })
    r2 = _run(build_p2, maps2)
    oT = np.concatenate([r["oT"] for r in r2], axis=1)

    wout0 = np.ascontiguousarray(L0_attn_w_out, dtype=f)
    lg = np.ascontiguousarray(lgT.T)
    r3 = _run(build_p3, [{"oT": np.ascontiguousarray(oT[:, c * TC:(c + 1) * TC]),
                          "wout": wout0,
                          "lg": np.ascontiguousarray(lg[c * TC:(c + 1) * TC])}
                         for c in range(NCORE)])
    sT = np.concatenate([r["sT"] for r in r3], axis=1)
    cmb0 = np.concatenate([r["cmb"] for r in r3], axis=0)

    c0T = np.ascontiguousarray(cmb0.T)
    cfc = np.ascontiguousarray(L0_ffn_c_fc, dtype=f)
    cpr = np.ascontiguousarray(L0_ffn_c_proj, dtype=f)
    r4 = _run(build_p4, [{"xnT": np.ascontiguousarray(xnT[:, c * TC:(c + 1) * TC]),
                          "cT": np.ascontiguousarray(c0T[:, c * TC:(c + 1) * TC]),
                          "cfc": cfc, "cpr": cpr} for c in range(NCORE)])
    gT = np.concatenate([r["gT"] for r in r4], axis=1)

    wqkv1T = np.ascontiguousarray(np.asarray(L1_attn_W_qkv, dtype=f).T)
    nsc = np.ascontiguousarray(L1_ffn_norm_scale, dtype=f)
    gwT = np.ascontiguousarray(np.asarray(L1_ffn_gate_w, dtype=f).T)
    gb = np.ascontiguousarray(L1_ffn_gate_b, dtype=f)
    r5 = _run(build_p5, [{"xT": s,
                          "sT": np.ascontiguousarray(sT[:, c * TC:(c + 1) * TC]),
                          "gT": np.ascontiguousarray(gT[:, c * TC:(c + 1) * TC]),
                          "wqkvT": wqkv1T, "nsc": nsc, "gwT": gwT, "gb": gb}
                         for c, s in enumerate(_cols(xT))])
    x1T = np.concatenate([r["x1T"] for r in r5], axis=1)
    xn1T = np.concatenate([r["xn1T"] for r in r5], axis=1)
    tT = np.concatenate([r["tT"] for r in r5], axis=1)
    glT = np.concatenate([r["glT"] for r in r5], axis=1)
    qkv1T = np.concatenate([r["qkv1T"] for r in r5], axis=1)

    Q1, K1, V1 = qkv1T[0:D], qkv1T[D:2 * D], qkv1T[2 * D:3 * D]
    maps6 = []
    for c in range(NCORE):
        b = c // 4
        K1b = K1[:, b * 8192:(b + 1) * 8192]
        V1b = np.ascontiguousarray(V1[:, b * 8192:(b + 1) * 8192].T)
        KWl, VWl, MKl = [], [], []
        for qb in range(4):
            r0 = (c % 4) * TC + qb * 512
            w0 = max(0, r0 - 512)
            kw = np.zeros((D, 1152), dtype=f)
            vw = np.zeros((1152, D), dtype=f)
            hi = min(8192, w0 + 1152)
            kw[:, :hi - w0] = K1b[:, w0:hi]
            vw[:hi - w0] = V1b[w0:hi]
            KWl.append(np.concatenate([K1b[:, 0:128], kw], axis=1))
            VWl.append(np.concatenate([V1b[0:128], vw], axis=0))
            m = np.full((10, 128, 512), -1e30, dtype=f)
            rr = r0 + np.arange(512)[None, :]
            pp = np.arange(128)[:, None]
            if w0 > 0:
                m[0][(pp < 16) & (rr >= -1)] = 0.0
            for kb in range(1, 10):
                kk = w0 + (kb - 1) * 128 + pp
                ok = (kk < 16) | ((kk >= 16) & (kk <= rr) & (kk >= rr - 512))
                m[kb][ok] = 0.0
            MKl.append(m)
        maps6.append({
            "qT": np.ascontiguousarray(Q1[:, c * TC:(c + 1) * TC]),
            "KW": np.stack(KWl), "VW": np.stack(VWl), "MK": np.stack(MKl)})
    r6 = _run(build_p6, maps6)
    o1T = np.concatenate([r["oT"] for r in r6], axis=1)

    wout1T = np.ascontiguousarray(np.asarray(L1_attn_W_out, dtype=f).T)
    gl = np.ascontiguousarray(glT.T)
    r7 = _run(build_p7, [{"oT": np.ascontiguousarray(o1T[:, c * TC:(c + 1) * TC]),
                          "wout": wout1T,
                          "gl": np.ascontiguousarray(gl[c * TC:(c + 1) * TC])}
                         for c in range(NCORE)])
    s1T = np.concatenate([r["sT"] for r in r7], axis=1)
    cmb1 = np.concatenate([r["cmb"] for r in r7], axis=0)

    c1T = np.ascontiguousarray(cmb1.T)
    w1 = np.asarray(L1_ffn_w1, dtype=f)
    w1A = np.ascontiguousarray(np.transpose(w1[:, 0::2, :], (0, 2, 1)))
    w1B = np.ascontiguousarray(np.transpose(w1[:, 1::2, :], (0, 2, 1)))
    b1 = np.asarray(L1_ffn_b1, dtype=f)
    b1A = np.ascontiguousarray(b1[:, 0::2])
    b1B = np.ascontiguousarray(b1[:, 1::2])
    w2T = np.ascontiguousarray(np.transpose(np.asarray(L1_ffn_w2, dtype=f),
                                            (0, 2, 1)))
    b2 = np.ascontiguousarray(L1_ffn_b2, dtype=f)
    r8 = _run(build_p8, [{"tT": np.ascontiguousarray(tT[:, c * TC:(c + 1) * TC]),
                          "cT": np.ascontiguousarray(c1T[:, c * TC:(c + 1) * TC]),
                          "w1A": w1A, "w1B": w1B, "b1A": b1A, "b1B": b1B,
                          "w2T": w2T, "b2": b2} for c in range(NCORE)])
    fT = np.concatenate([r["fT"] for r in r8], axis=1)

    r9 = _run(build_p9, [{"x1T": np.ascontiguousarray(x1T[:, c * TC:(c + 1) * TC]),
                          "s1T": np.ascontiguousarray(s1T[:, c * TC:(c + 1) * TC]),
                          "xn1T": np.ascontiguousarray(xn1T[:, c * TC:(c + 1) * TC]),
                          "fT": np.ascontiguousarray(fT[:, c * TC:(c + 1) * TC])}
                         for c in range(NCORE)])
    x2T = np.concatenate([r["x2T"] for r in r9], axis=1)
    return np.ascontiguousarray(x2T.T).reshape(2, 8192, 1024).astype(np.float32)
